# version 01 (final)
# baseline (speedup 1.0000x reference)
import numpy as np
from contextlib import ExitStack
import concourse.bass as bass
import concourse.mybir as mybir

F32 = mybir.dt.float32
BF16 = mybir.dt.bfloat16
AF = mybir.ActivationFunctionType
ALU = mybir.AluOpType
AX = mybir.AxisListType

S, D, NT, NB = 4096, 1024, 32, 8
H, HD, NKV, FFN, E = 16, 64, 4, 2048, 8
EPS = 1e-5
ENGS = ("sync", "gpsimd", "tensor", "scalar", "vector")
TWO_PI = float(2.0 * np.pi)
PI = float(np.pi)

QLO = (0, 1, 2, 3, 8, 9, 10, 11)
QHI = (4, 5, 6, 7, 12, 13, 14, 15)
QCOL = {}
for _pr in range(8):
    QCOL[QLO[_pr]] = 128 * _pr
    QCOL[QHI[_pr]] = 128 * _pr + 64


class Prog:
    def __init__(self, name, cnt):
        self.name = name
        self.ops = {e: [] for e in ENGS}
        self.cnt = cnt

    def val(self, s):
        return self.cnt.get(s, 0)

    def add(self, eng, fn, waits=(), inc=None, inc2=None):
        w = tuple((s, v) for (s, v) in waits if v > 0)
        incs = []
        for pair in (inc, inc2):
            if pair:
                s, k = pair
                self.cnt[s] = self.cnt.get(s, 0) + k
                incs.append((s, k))
        self.ops[eng].append((w, fn, tuple(incs)))
        return self.cnt[incs[0][0]] if incs else None


def emit(nc, prog, sems):
    with nc.Block(prog.name) as blk:
        for engname in ENGS:
            ops = prog.ops[engname]
            if not ops:
                continue

            def make(ops):
                def run(eng):
                    for waits, fn, incs in ops:
                        for s, v in waits:
                            eng.wait_ge(sems[s], v)
                        instr = fn(eng)
                        for s, k in incs:
                            instr = instr.then_inc(sems[s], k)
                return run

            getattr(blk, engname)(make(ops))


def bc(t, off, nh, w):
    row = t.shape[-1]
    return bass.AP(t, off, [[row, 128], [0, nh], [1, w]])


def h3(t, c=64):
    return t.ap().rearrange("p (h c) -> p h c", c=c)


def v65(tens, kc, head, nper=H):
    return tens.ap().rearrange("p (t h c) -> p t h c", h=nper, c=65)[:, kc, head, :]


def build(debug=(), nphase=99):
    nc = bass.Bass("TRN2", target_bir_lowering=False)

    x_h = nc.dram_tensor("x", [S, D], F32, kind="ExternalInput")
    wqkvT_h = nc.dram_tensor("wqkvT", [D, 1536], F32, kind="ExternalInput")
    woT_h = nc.dram_tensor("woT", [D, D], F32, kind="ExternalInput")
    gateT_h = nc.dram_tensor("gateT", [D, E], F32, kind="ExternalInput")
    w13T_h = nc.dram_tensor("w13T", [E, D, 2 * FFN], F32, kind="ExternalInput")
    w2T_h = nc.dram_tensor("w2T", [E, FFN, D], F32, kind="ExternalInput")
    ln1c_h = nc.dram_tensor("ln1c", [128, 8], F32, kind="ExternalInput")
    ln2b_h = nc.dram_tensor("ln2b", [128, D], F32, kind="ExternalInput")
    id32_h = nc.dram_tensor("id32", [128, 128], F32, kind="ExternalInput")
    triu_h = nc.dram_tensor("triu", [128, 128], BF16, kind="ExternalInput")
    sel_h = nc.dram_tensor("sel", [8, 1024], F32, kind="ExternalInput")
    iotar_h = nc.dram_tensor("iotar", [128, 32], F32, kind="ExternalInput")
    iotac_h = nc.dram_tensor("iotac", [128, 1], F32, kind="ExternalInput")
    out_h = nc.dram_tensor("out", [S, D], F32, kind="ExternalOutput")

    def scratch(name, shape, dt):
        return nc.dram_tensor(name, shape, dt,
                              kind="ExternalOutput" if name in debug else "Internal")

    xnbf_s = scratch("xnbf", [S, D], BF16)
    xnf32_s = scratch("xnf32", [S, D], F32)
    h1_s = scratch("h1", [S, D], BF16)
    q_s = scratch("q", [S, D], BF16)
    k_s = scratch("k", [S, NKV * HD], BF16)
    acc1_s = scratch("acc1", [S, D], F32)
    attn2_s = scratch("attn2", [S, D], BF16)
    xf_s = scratch("xf", [S, D], F32)
    h2_s = scratch("h2", [S, D], BF16)
    wfr_s = scratch("wfr", [E, S], F32)
    stats_s = scratch("stats", [S, 12], F32)
    tabs_s = scratch("tabs", [128, 2 * NT * 32], F32)

    es = ExitStack()
    sems = {}
    cnt = {}

    with es:
        for s_ in ("xld", "wld", "st", "mm", "ex", "pv", "ev", "tp", "rp",
                   "act", "dve", "g1", "cvt", "rc", "asm", "ot", "gg"):
            sems[s_] = es.enter_context(nc.semaphore(s_))

        def sb(name, shape, dt):
            return es.enter_context(nc.sbuf_tensor(name, shape, dt))

        id32 = sb("id32s", [128, 128], F32)
        triu = sb("trius", [128, 128], BF16)
        ln1c = sb("ln1cs", [128, 8], F32)
        ln2b = sb("ln2bs", [128, D], F32)
        sel = sb("sels", [8, 1024], F32)
        epsc = sb("epsc", [128, 1], F32)
        junk = sb("junk", [128, 512], F32)
        es2 = ExitStack()
        xn_tm = es2.enter_context(nc.sbuf_tensor("xn_tm", [128, NT * H * 65], BF16))
        v_tm = es2.enter_context(nc.sbuf_tensor("v_tm", [128, NT * NKV * 65], BF16))

        def warm(p, func):
            p.add("scalar", lambda e: e.activation(junk[:, :], junk[:, :], func))
            p.add("scalar", lambda e: e.activation(junk[:, :], junk[:, :], func),
                  inc=("act", 1))

        p = Prog("init", cnt)
        for dst, src in ((id32, id32_h), (triu, triu_h), (ln1c, ln1c_h),
                         (ln2b, ln2b_h), (sel, sel_h)):
            p.add("sync", lambda e, d=dst, s=src: e.dma_start(d[:, :], s.ap()), inc=("xld", 16))
        xn_ones = xn_tm.ap().rearrange("p (a c) -> p a c", c=65)[:, :, 64:65]
        v_ones = v_tm.ap().rearrange("p (a c) -> p a c", c=65)[:, :, 64:65]
        p.add("vector", lambda e: e.memset(xn_ones, 1.0), inc=("dve", 1))
        p.add("vector", lambda e: e.memset(v_ones, 1.0), inc=("dve", 1))
        p.add("vector", lambda e: e.memset(epsc[:, :], EPS), inc=("dve", 1))
        p.add("sync", lambda e: None, waits=[("xld", p.val("xld"))])
        emit(nc, p, sems)
        if nphase < 1:
            return nc

        with ExitStack() as ps:
            def sbl(n, sh, dt):
                return ps.enter_context(nc.sbuf_tensor(n, sh, dt))
            xb = [sbl(f"xb{i}", [128, D], F32) for i in range(3)]
            sqb = [sbl(f"sqb{i}", [128, D], F32) for i in range(2)]
            xnf = [sbl(f"xnf{i}", [128, D], F32) for i in range(2)]
            xnb = [sbl(f"xnb{i}", [128, D], BF16) for i in range(2)]
            h1b = [sbl(f"h1b{i}", [128, D], BF16) for i in range(2)]
            st1 = sbl("st1", [128, 12], F32)
            p = Prog("P0a", cnt)
            warm(p, AF.Sqrt)
            act_after, st_after, ev_after0a, dve_after = [], [], [], []
            for t in range(NT):
                b = t % 2
                xt = xb[t % 3]
                p.add("sync", lambda e, xt=xt, t=t: e.dma_start(
                    xt[:, :], x_h.ap()[t * 128:(t + 1) * 128, :]),
                    waits=[("dve", dve_after[t - 3] if t >= 3 else 0)], inc=("xld", 16))
                xldv = p.val("xld")
                p.add("vector", lambda e, xt=xt: e.reduce_sum(st1[:, 0:1], xt[:, :], axis=AX.X),
                      waits=[("xld", xldv)])
                p.add("scalar", lambda e, xt=xt, sq=sqb[b]: e.activation(
                    sq[:, :], xt[:, :], AF.Square),
                    waits=[("xld", xldv), ("dve", dve_after[t - 2] if t >= 2 else 0)],
                    inc=("act", 1))
                avq = p.val("act")

                def ds(fn, waits=()):
                    w = [("dve", p.val("dve"))] + list(waits)
                    p.add("vector", fn, waits=w, inc=("dve", 1))
                ds(lambda e, sq=sqb[b]: e.reduce_sum(st1[:, 1:2], sq[:, :], axis=AX.X),
                   waits=[("act", avq)])
                ds(lambda e: e.tensor_scalar_mul(st1[:, 1:2], st1[:, 1:2], 1.0 / D))
                ds(lambda e: e.tensor_scalar_mul(st1[:, 2:3], st1[:, 0:1], 1.0 / D))
                ds(lambda e: e.tensor_scalar_mul(st1[:, 3:4], st1[:, 0:1], -1.0 / D))
                ds(lambda e: e.tensor_mul(st1[:, 4:5], st1[:, 2:3], st1[:, 2:3]))
                ds(lambda e: e.tensor_sub(st1[:, 4:5], st1[:, 1:2], st1[:, 4:5]))
                dv = p.val("dve")
                p.add("scalar", lambda e: e.activation(st1[:, 5:6], st1[:, 4:5], AF.Sqrt, bias=epsc[:, 0:1]),
                      waits=[("dve", dv)], inc=("act", 1))
                av = p.val("act")
                ds(lambda e: e.reciprocal(st1[:, 6:7], st1[:, 5:6]), waits=[("act", av)])
                ds(lambda e: e.tensor_scalar_add(st1[:, 8:9], st1[:, 4:5], EPS))
                ds(lambda e: e.reciprocal(st1[:, 7:8], st1[:, 8:9]))
                ds(lambda e: e.tensor_mul(st1[:, 9:10], st1[:, 4:5], st1[:, 7:8]))
                dv = p.val("dve")
                p.add("scalar", lambda e: e.activation(st1[:, 10:11], st1[:, 9:10], AF.Sqrt,
                                                       bias=epsc[:, 0:1]),
                      waits=[("dve", dv)], inc=("act", 1))
                av = p.val("act")
                ds(lambda e: e.reciprocal(st1[:, 11:12], st1[:, 10:11]), waits=[("act", av)])
                dv = p.val("dve")
                stw = st_after[t - 2] if t >= 2 else 0
                p.add("vector", lambda e, xt=xt, o=xnf[b]: e.tensor_scalar(
                    o[:, :], xt[:, :], st1[:, 3:4], st1[:, 6:7], ALU.add, ALU.mult),
                    waits=[("dve", dv), ("st", stw),
                           ("act", act_after[t - 2] if t >= 2 else 0),
                           ("ev", ev_after0a[t - 2] if t >= 2 else 0)], inc=("dve", 1))
                dvxn = p.val("dve")
                p.add("vector", lambda e, o=h1b[b], i=xnf[b]: e.tensor_scalar_mul(
                    o[:, :], i[:, :], st1[:, 11:12]), waits=[("dve", dvxn)], inc=("dve", 1))
                dvx = p.val("dve")
                p.add("scalar", lambda e, xt=xt, o=xnb[b], i=xnf[b]: e.activation(
                    o[:, :], i[:, :], AF.Copy), waits=[("dve", dvx)], inc=("act", 1))
                av = p.val("act")
                act_after.append(av)
                dve_after.append(dvx)
                dst = xn_tm.ap().rearrange("p (t h c) -> p t h c", h=H, c=65)[:, t, :, 0:64]
                p.add("vector", lambda e, i=xnb[b], dst=dst: e.tensor_copy(dst, h3(i)),
                      waits=[("act", av)], inc=("ev", 1))
                ev_after0a.append(p.val("ev"))
                p.add("sync", lambda e, i=xnf[b], t=t: e.dma_start(
                    xnf32_s.ap()[t * 128:(t + 1) * 128, :], i[:, :]),
                    waits=[("dve", dvx)], inc=("st", 16))
                p.add("sync", lambda e, i=xnb[b], t=t: e.dma_start(
                    xnbf_s.ap()[t * 128:(t + 1) * 128, :], i[:, :]), inc=("st", 16))
                p.add("sync", lambda e, i=h1b[b], t=t: e.dma_start(
                    h1_s.ap()[t * 128:(t + 1) * 128, :], i[:, :]), inc=("st", 16))
                if "stats" in debug:
                    p.add("sync", lambda e, t=t: e.dma_start(
                        stats_s.ap()[t * 128:(t + 1) * 128, :], st1[:, :]), inc=("st", 16))
                st_after.append(p.val("st"))
            p.add("sync", lambda e: e.nop(), waits=[("st", p.val("st"))])
            emit(nc, p, sems)
        if nphase < 2:
            return nc

        with ExitStack() as ps:
            def sbl(n, sh, dt):
                return ps.enter_context(nc.sbuf_tensor(n, sh, dt))
            wqkv = sbl("wqkv", [128, 8 * 1536], BF16)
            cost = sbl("cost", [128, NT * 32], F32)
            sint = sbl("sint", [128, NT * 32], F32)
            h1f = [sbl(f"h1f{i}", [128, 8 * 128], BF16) for i in range(3)]
            qrb = [sbl(f"qrb{i}", [128, D], BF16) for i in range(2)]
            krb = [sbl(f"krb{i}", [128, 256], BF16) for i in range(2)]
            rtmp = sbl("rtmp", [128, 512], F32)
            angv = sbl("angv", [128, 64], F32)
            angq = sbl("angq", [128, 64], F32)
            angi = sbl("angi", [128, 64], mybir.dt.int32)
            thb = sbl("thb", [128, 32], F32)
            ioc = sbl("ioc", [128, 2], F32)
            psq = [ps.enter_context(nc.psum_tensor(f"psq{i}", [128, 512], F32))
                   for i in range(4)]
            pskv = [ps.enter_context(nc.psum_tensor(f"pskv{i}", [128, 512], F32))
                    for i in range(2)]
            psqs = [[psq[0], psq[1]], [psq[2], psq[3]]]

            p = Prog("P0b", cnt)
            for dc in range(8):
                p.add("gpsimd", lambda e, dc=dc: e.dma_start(
                    wqkv[:, dc * 1536:(dc + 1) * 1536],
                    wqkvT_h.ap()[dc * 128:(dc + 1) * 128, :]), inc=("wld", 16))
                wv = p.val("wld")
                p.add("vector", lambda e, dc=dc: e.tensor_scalar_mul(
                    wqkv[:, dc * 1536:(dc + 1) * 1536],
                    wqkv[:, dc * 1536:(dc + 1) * 1536], ln1c[:, dc:dc + 1]),
                    waits=[("wld", wv)], inc=("cvt", 1))
            cvtv = p.val("cvt")
            p.add("sync", lambda e: e.dma_start(thb[:, :], iotar_h.ap()), inc=("g1", 16))
            p.add("sync", lambda e: e.dma_start(ioc[:, 0:1], iotac_h.ap()), inc=("g1", 16))
            g1v = p.val("g1")
            warm(p, AF.Exp)
            p.add("scalar", lambda e: e.activation(thb[:, :], thb[:, :], AF.Exp,
                                                   scale=float(-np.log(10000.0) / 32.0)),
                  waits=[("g1", g1v)], inc=("act", 1))
            warm(p, AF.Sin)
            tab_act0 = p.val("act")
            for t in range(NT):

                def dsb(fn, waits=()):
                    w = [("dve", p.val("dve"))] + list(waits)
                    p.add("vector", fn, waits=w, inc=("dve", 1))
                dsb(lambda e, t=t: e.tensor_scalar_add(ioc[:, 1:2], ioc[:, 0:1],
                                                       float(128 * t)),
                    waits=[("g1", g1v), ("act", tab_act0)])
                dsb(lambda e: e.tensor_scalar_mul(angv[:, 0:32], thb[:, :], ioc[:, 1:2]))
                dsb(lambda e: e.tensor_single_scalar(
                    angv[:, 32:64], angv[:, 0:32], PI / 2.0, ALU.add))
                dsb(lambda e: e.tensor_scalar_mul(angq[:, :], angv[:, :], 1.0 / TWO_PI))
                dsb(lambda e: e.tensor_copy(angi[:, :], angq[:, :]))
                dsb(lambda e: e.tensor_copy(angq[:, :], angi[:, :]))
                dsb(lambda e: e.scalar_tensor_tensor(
                    angv[:, :], angq[:, :], -TWO_PI, angv[:, :], ALU.mult, ALU.add))
                dsb(lambda e: e.tensor_single_scalar(rtmp[:, 0:64], angv[:, :], PI,
                                                     ALU.is_gt))
                dsb(lambda e: e.scalar_tensor_tensor(
                    angv[:, :], rtmp[:, 0:64], -TWO_PI, angv[:, :], ALU.mult, ALU.add))
                dsb(lambda e: e.tensor_single_scalar(rtmp[:, 0:64], angv[:, :], -PI,
                                                     ALU.is_lt))
                dsb(lambda e: e.scalar_tensor_tensor(
                    angv[:, :], rtmp[:, 0:64], TWO_PI, angv[:, :], ALU.mult, ALU.add))
                dv = p.val("dve")
                p.add("scalar", lambda e, t=t: e.activation(
                    sint[:, 32 * t:32 * (t + 1)], angv[:, 0:32], AF.Sin),
                    waits=[("dve", dv)])
                p.add("scalar", lambda e, t=t: e.activation(
                    cost[:, 32 * t:32 * (t + 1)], angv[:, 32:64], AF.Sin), inc=("act", 1))
            tab_act = p.val("act")

            mm_after, rp_after, ev_after, st_after = [], [], [], []
            for t in range(NT):
                hb = h1f[t % 3]
                for dc in range(8):
                    p.add("sync", lambda e, hb=hb, t=t, dc=dc: e.dma_start(
                        hb[:, dc * 128:(dc + 1) * 128],
                        h1_s.ap()[t * 128:(t + 1) * 128, dc * 128:(dc + 1) * 128],
                        transpose=True),
                        waits=[("mm", mm_after[t - 3] if t >= 3 else 0)], inc=("xld", 16))
                xldv = p.val("xld")
                qp = psqs[t % 2]
                kvp = pskv[t % 2]
                w0 = [("xld", xldv), ("cvt", cvtv), ("act", tab_act),
                      ("rp", rp_after[t - 2] if t >= 2 else 0),
                      ("ev", ev_after[t - 2] if t >= 2 else 0)]
                for dc in range(8):
                    for j, (pst, c0) in enumerate(((qp[0], 0), (qp[1], 512), (kvp, 1024))):
                        p.add("tensor", lambda e, pst=pst, dc=dc, c0=c0, hb=hb: e.matmul(
                            pst[:, 0:512], hb[:, dc * 128:(dc + 1) * 128],
                            wqkv[:, dc * 1536 + c0: dc * 1536 + c0 + 512],
                            start=(dc == 0), stop=(dc == 7)),
                            waits=(w0 if (dc == 0 and j == 0) else []),
                            inc=(("mm", 1) if dc == 7 else None))
                mm_after.append(p.val("mm"))
                mmv = p.val("mm")
                qo, ko = qrb[t % 2], krb[t % 2]
                co = 32 * t

                def rope_grp_ops(psrc, hsrc0, nh, dstt, dstbase, dststep, half, co):
                    s1 = h3(psrc)[:, hsrc0:hsrc0 + nh,
                                  (0 if half == 0 else 32):(32 if half == 0 else 64)]
                    s2 = h3(psrc)[:, hsrc0:hsrc0 + nh,
                                  (32 if half == 0 else 0):(64 if half == 0 else 32)]
                    row = dstt.shape[-1]
                    dst = bass.AP(dstt, dstbase + half * 32, [[row, 128], [dststep, nh], [1, 32]])
                    cba = bc(cost, co, nh, 32)
                    sba = bc(sint, co, nh, 32)
                    rt = h3(rtmp, c=32)[:, 0:nh, :]
                    op3 = ALU.subtract if half == 0 else ALU.add
                    return (lambda e: e.tensor_tensor(rt, s2, sba, ALU.mult),
                            lambda e: e.tensor_tensor(dst, s1, cba, ALU.mult),
                            lambda e, op3=op3: e.tensor_tensor(dst, dst, rt, op3))

                def add_rope(psrc, hsrc0, nh, dstt, dstbase, dststep, half, w0=()):
                    f1, f2, f3 = rope_grp_ops(psrc, hsrc0, nh, dstt, dstbase, dststep,
                                              half, co)
                    p.add("vector", f1, waits=list(w0), inc=("rp", 1))
                    p.add("vector", f2, inc=("rp", 1))
                    rv = p.val("rp")
                    p.add("vector", f3, waits=[("rp", rv)], inc=("rp", 1))

                stw = st_after[t - 2] if t >= 2 else 0
                add_rope(qp[0], 0, 4, qo, 0, 128, 0, w0=[("mm", mmv), ("st", stw)])
                add_rope(qp[0], 0, 4, qo, 0, 128, 1)
                add_rope(qp[0], 4, 4, qo, 64, 128, 0)
                add_rope(qp[0], 4, 4, qo, 64, 128, 1)
                add_rope(qp[1], 0, 4, qo, 512, 128, 0)
                add_rope(qp[1], 0, 4, qo, 512, 128, 1)
                add_rope(qp[1], 4, 4, qo, 576, 128, 0)
                add_rope(qp[1], 4, 4, qo, 576, 128, 1)
                add_rope(kvp, 0, 4, ko, 0, 64, 0)
                add_rope(kvp, 0, 4, ko, 0, 64, 1)
                rp_after.append(p.val("rp"))
                rpv = p.val("rp")
                vdst = v_tm.ap().rearrange("p (t g c) -> p t g c", g=NKV, c=65)[:, t, :, 0:64]
                p.add("vector", lambda e, kvp=kvp, vdst=vdst: e.tensor_copy(
                    vdst, h3(kvp)[:, 4:8, :]), inc=("ev", 1))
                ev_after.append(p.val("ev"))
                p.add("sync", lambda e, qo=qo, t=t: e.dma_start(
                    q_s.ap()[t * 128:(t + 1) * 128, :], qo[:, :]),
                    waits=[("rp", rpv)], inc=("st", 16))
                p.add("sync", lambda e, ko=ko, t=t: e.dma_start(
                    k_s.ap()[t * 128:(t + 1) * 128, :], ko[:, :]), inc=("st", 16))
                st_after.append(p.val("st"))
            if "tabs" in debug:
                p.add("sync", lambda e: e.dma_start(tabs_s.ap()[:, 0:NT * 32], cost[:, :]),
                      inc=("st", 16))
                p.add("sync", lambda e: e.dma_start(tabs_s.ap()[:, NT * 32:], sint[:, :]),
                      inc=("st", 16))
            p.add("sync", lambda e: e.nop(), waits=[("st", p.val("st"))])
            emit(nc, p, sems)
        if nphase < 3:
            return nc

        def attention(p, tag, QF, KF, VT, vheads, HEADSETS, outbf, out_s, addx):
            ptb = {}
            with ExitStack() as ps2:
                def sbl2(n, sh, dt):
                    return ps2.enter_context(nc.sbuf_tensor(n, sh, dt))
                for r in range(4):
                    nbuf = 2 if r == 0 else 1
                    ptb[r] = [sbl2(f"{tag}pt{r}_{i}", [128, 1024], BF16) for i in range(nbuf)]
                ub = [sbl2(f"{tag}ub{i}", [65, 512], F32) for i in range(2)]
                recb = sbl2(f"{tag}recb", [128, 2], F32)
                abuf = sbl2(f"{tag}abuf", [128, S], F32 if addx else BF16)
                xtb = [sbl2(f"{tag}xtb{i}", [128, D], F32) for i in range(2)] if addx else None
                a1b = [sbl2(f"{tag}a1b{i}", [128, D], F32) for i in range(2)] if addx else None
                psp = [ps2.enter_context(nc.psum_tensor(f"{tag}psp{i}", [128, 1024], F32))
                       for i in range(2)]
                pvp = [ps2.enter_context(nc.psum_tensor(f"{tag}pvp{i}", [65, 512], F32))
                       for i in range(2)]
                ptp = [ps2.enter_context(nc.psum_tensor(f"{tag}ptp{i}", [128, 65], F32))
                       for i in range(2)]

                for r in range(1, 4):
                    p.add("vector", lambda e, r=r: e.memset(ptb[r][0][:, 0:128 * r], 0.0))
                    p.add("vector", lambda e, r=r: e.memset(ptb[r][0][:, 512:512 + 128 * r], 0.0),
                          inc=("dve", 1))
                zv = p.val("dve")
                ptfree = {}
                spfree = [0, 0]
                pvfree = [0, 0]
                tppfree = [0, 0]
                ub_free = [0, 0]
                abuf_free = 0
                st_abuf = 0
                r0_i = [0]
                for J in range(NB):
                    for (kT, hA, hB) in HEADSETS:
                        nkc = 4 * J + 4
                        for kc in range(nkc):
                            r = kc - 4 * J
                            sp = psp[kc % 2]
                            if r <= 0:
                                buf = ptb[0][r0_i[0]]
                                r0_i[0] = (r0_i[0] + 1) % 2
                                c0 = 0
                            else:
                                buf = ptb[r][0]
                                c0 = 128 * r
                            w = [("ex", spfree[kc % 2])]
                            qA_t, qA_p, qA_c = QF(hA)
                            qB_t, qB_p, qB_c = QF(hB)
                            p.add("tensor", lambda e, sp=sp, kT=kT, kc=kc, J=J, c0=c0,
                                  qt=qA_t, qp_=qA_p, qc=qA_c: e.matmul(
                                sp[:, c0:512], kT[0:64, kc * 128:(kc + 1) * 128],
                                qt[qp_:qp_ + 64, qc + 512 * J + c0: qc + 512 * (J + 1)],
                                start=True, stop=True), waits=w)
                            p.add("tensor", lambda e, sp=sp, kT=kT, kc=kc, J=J, c0=c0,
                                  qt=qB_t, qp_=qB_p, qc=qB_c: e.matmul(
                                sp[:, 512 + c0:1024], kT[64:128, kc * 128:(kc + 1) * 128],
                                qt[qp_:qp_ + 64, qc + 512 * J + c0: qc + 512 * (J + 1)],
                                start=True, stop=True), inc=("mm", 1))
                            mmv = p.val("mm")
                            spfree[kc % 2] = p.val("ex") + (1 if r < 0 else 2)
                            we = [("mm", mmv), ("pv", ptfree.get(id(buf), 0)), ("dve", zv)]
                            if r < 0:
                                p.add("scalar", lambda e, sp=sp, buf=buf: e.activation(
                                    buf[:, :], sp[:, :], AF.Exp, scale=0.125),
                                    waits=we, inc=("ex", 1))
                            else:
                                p.add("scalar", lambda e, sp=sp, buf=buf, c0=c0: e.activation(
                                    buf[:, c0:512], sp[:, c0:512], AF.Exp, scale=0.125),
                                    waits=we, inc=("ex", 1))
                                p.add("scalar", lambda e, sp=sp, buf=buf, c0=c0: e.activation(
                                    buf[:, 512 + c0:1024], sp[:, 512 + c0:1024], AF.Exp,
                                    scale=0.125), inc=("ex", 1))
                            exv = p.val("ex")
                            if r >= 0:
                                p.add("vector", lambda e, buf=buf, c0=c0: e.tensor_tensor(
                                    buf[:, c0:c0 + 128], buf[:, c0:c0 + 128], triu[:, :],
                                    ALU.mult), waits=[("ex", exv)])
                                p.add("vector", lambda e, buf=buf, c0=c0: e.tensor_tensor(
                                    buf[:, 512 + c0:512 + c0 + 128],
                                    buf[:, 512 + c0:512 + c0 + 128], triu[:, :], ALU.mult),
                                    inc=("dve", 1))
                                wpv = [("dve", p.val("dve"))]
                            else:
                                wpv = [("ex", exv)]
                            if kc == 0:
                                wpv = wpv + [("ev", pvfree[0]), ("ev", pvfree[1])]
                            first, last = (kc == 0), (kc == nkc - 1)
                            vA = v65(VT[0], kc, vheads[hA], VT[1])
                            vB = v65(VT[0], kc, vheads[hB], VT[1])
                            p.add("tensor", lambda e, buf=buf, vA=vA, first=first, last=last:
                                  e.matmul(pvp[0][:, :], vA, buf[:, 0:512],
                                           start=first, stop=last), waits=wpv)
                            p.add("tensor", lambda e, buf=buf, vB=vB, first=first, last=last:
                                  e.matmul(pvp[1][:, :], vB, buf[:, 512:1024],
                                           start=first, stop=last), inc=("pv", 1))
                            ptfree[id(buf)] = p.val("pv")
                        pvv = p.val("pv")
                        for hh, head in enumerate((hA, hB)):
                            u = ub[hh]
                            p.add("vector", lambda e, u=u, hh=hh: e.tensor_copy(
                                u[:, :], pvp[hh][:, :]),
                                waits=[("pv", pvv), ("tp", ub_free[hh])], inc=("ev", 1))
                            pvfree[hh] = p.val("ev")
                            evv = p.val("ev")
                            for sub in range(4):
                                tpp = ptp[sub % 2]
                                p.add("tensor", lambda e, u=u, sub=sub, tpp=tpp: e.transpose(
                                    tpp[:, :], u[0:65, sub * 128:(sub + 1) * 128],
                                    id32[0:65, 0:65]),
                                    waits=[("ev", evv), ("rc", tppfree[sub % 2])],
                                    inc=("tp", 1))
                                tpv = p.val("tp")
                                p.add("vector", lambda e, tpp=tpp, hh=hh: e.reciprocal(
                                    recb[:, hh:hh + 1], tpp[:, 64:65]),
                                    waits=[("tp", tpv)], inc=("rc", 1))
                                rcv = p.val("rc")
                                dcol = (sub * D) + head * 64
                                wts = [("rc", rcv)]
                                if sub == 0 and hh == 0:
                                    wts.append(("dve" if addx else "st",
                                                abuf_free if addx else st_abuf))
                                p.add("vector", lambda e, tpp=tpp, dcol=dcol, hh=hh:
                                      e.tensor_scalar_mul(abuf[:, dcol:dcol + 64],
                                                          tpp[:, 0:64], recb[:, hh:hh + 1]),
                                      waits=wts, inc=("rc", 1))
                                tppfree[sub % 2] = p.val("rc")
                            ub_free[hh] = p.val("tp")
                    rcv = p.val("rc")
                    if addx:
                        for sub in range(4):
                            t = 4 * J + sub
                            xt = xtb[sub % 2]
                            p.add("sync", lambda e, xt=xt, t=t: e.dma_start(
                                xt[:, :], x_h.ap()[t * 128:(t + 1) * 128, :]),
                                waits=[("dve", max(0, p.val("dve") - 4))], inc=("xld", 16))
                            xv2 = p.val("xld")
                            ob = a1b[sub % 2]
                            p.add("vector", lambda e, xt=xt, ob=ob, sub=sub: e.tensor_add(
                                ob[:, :], xt[:, :], abuf[:, sub * D:(sub + 1) * D]),
                                waits=[("rc", rcv), ("xld", xv2),
                                       ("st", max(0, p.val("st") - 32))], inc=("dve", 1))
                            dvv = p.val("dve")
                            p.add("sync", lambda e, ob=ob, t=t: e.dma_start(
                                out_s.ap()[t * 128:(t + 1) * 128, :], ob[:, :]),
                                waits=[("dve", dvv)], inc=("st", 16))
                        abuf_free = p.val("dve")
                    else:
                        for sub in range(4):
                            t = 4 * J + sub
                            p.add("sync", lambda e, t=t, sub=sub: e.dma_start(
                                out_s.ap()[t * 128:(t + 1) * 128, :],
                                abuf[:, sub * D:(sub + 1) * D]),
                                waits=[("rc", rcv)], inc=("st", 16))
                        st_abuf = p.val("st")
                p.add("sync", lambda e: e.nop(), waits=[("st", p.val("st"))])

        with ExitStack() as ps:
            def sbl(n, sh, dt):
                return ps.enter_context(nc.sbuf_tensor(n, sh, dt))
            xnf_ = [sbl(f"xnfm{i}", [128, S], BF16) for i in range(8)]
            p = Prog("P1", cnt)
            warm(p, AF.Exp)
            for c in range(8):
                p.add("sync", lambda e, c=c: e.dma_start(
                    xnf_[c][:, :], xnbf_s.ap()[:, c * 128:(c + 1) * 128], transpose=True),
                    inc=("xld", 16))
            xldv = p.val("xld")
            p.add("tensor", lambda e: e.nop(), waits=[("xld", xldv)])

            def QF1(h):
                return (xnf_[h // 2], (h % 2) * 64, 0)
            HEADSETS1 = [(xnf_[c], 2 * c, 2 * c + 1) for c in range(8)]
            attention(p, "a", QF1, None, (xn_tm, H), {h: h for h in range(H)},
                      HEADSETS1, None, acc1_s, True)
            emit(nc, p, sems)
        if nphase < 4:
            return nc

        with ExitStack() as ps:
            def sbl(n, sh, dt):
                return ps.enter_context(nc.sbuf_tensor(n, sh, dt))
            qf = [sbl(f"qfm{i}", [128, S], BF16) for i in range(8)]
            kf = [sbl(f"kfm{i}", [128, S], BF16) for i in range(2)]
            p = Prog("P2", cnt)
            warm(p, AF.Exp)
            for c in range(8):
                p.add("sync", lambda e, c=c: e.dma_start(
                    qf[c][:, :], q_s.ap()[:, c * 128:(c + 1) * 128], transpose=True),
                    inc=("xld", 16))
            for i in range(2):
                p.add("sync", lambda e, i=i: e.dma_start(
                    kf[i][:, :], k_s.ap()[:, i * 128:(i + 1) * 128], transpose=True),
                    inc=("xld", 16))
            xldv = p.val("xld")
            p.add("tensor", lambda e: e.nop(), waits=[("xld", xldv)])

            def QF2(h):
                base = QCOL[h]
                return (qf[base // 128], base % 128, 0)
            HEADSETS2 = [(kf[0] if pr < 4 else kf[1], QLO[pr], QHI[pr]) for pr in range(8)]
            attention(p, "b", QF2, None, (v_tm, NKV), {h: h // 4 for h in range(H)},
                      HEADSETS2, None, attn2_s, False)
            emit(nc, p, sems)
        es2.close()
        if nphase < 5:
            return nc

        with ExitStack() as ps:
            def sbl(n, sh, dt):
                return ps.enter_context(nc.sbuf_tensor(n, sh, dt))
            wo = sbl("wo", [128, 8 * D], BF16)
            a2f = [sbl(f"a2f{i}", [128, 8 * 128], BF16) for i in range(3)]
            xnt = [sbl(f"xnt{i}", [128, D], F32) for i in range(2)]
            xft = [sbl(f"xft{i}", [128, D], F32) for i in range(2)]
            sqt = [sbl(f"sqt{i}", [128, D], F32) for i in range(2)]
            h2t = [sbl(f"h2t{i}", [128, D], BF16) for i in range(2)]
            st2 = sbl("st2", [128, 4], F32)
            psP = [ps.enter_context(nc.psum_tensor(f"psP{i}", [128, 1024], F32))
                   for i in range(2)]
            p = Prog("P3", cnt)
            warm(p, AF.Sqrt)
            for oc in range(8):
                p.add("gpsimd", lambda e, oc=oc: e.dma_start(
                    wo[:, oc * D:(oc + 1) * D], woT_h.ap()[oc * 128:(oc + 1) * 128, :]),
                    inc=("wld", 16))
            wldv = p.val("wld")
            mm_after, st_after, ev_after3 = [], [], []
            for t in range(NT):
                ab = a2f[t % 3]
                for oc in range(8):
                    p.add("sync", lambda e, ab=ab, t=t, oc=oc: e.dma_start(
                        ab[:, oc * 128:(oc + 1) * 128],
                        attn2_s.ap()[t * 128:(t + 1) * 128, oc * 128:(oc + 1) * 128],
                        transpose=True),
                        waits=[("mm", mm_after[t - 3] if t >= 3 else 0)], inc=("xld", 16))
                p.add("sync", lambda e, xn=xnt[t % 2], t=t: e.dma_start(
                    xn[:, :], xnf32_s.ap()[t * 128:(t + 1) * 128, :]),
                    waits=[("ev", ev_after3[t - 2] if t >= 2 else 0)], inc=("xld", 16))
                xldv = p.val("xld")
                pp = psP[t % 2]
                for oc in range(8):
                    for hb in range(2):
                        p.add("tensor", lambda e, pp=pp, ab=ab, oc=oc, hb=hb: e.matmul(
                            pp[:, hb * 512:(hb + 1) * 512], ab[:, oc * 128:(oc + 1) * 128],
                            wo[:, oc * D + hb * 512: oc * D + (hb + 1) * 512],
                            start=(oc == 0), stop=(oc == 7)),
                            waits=([("xld", xldv), ("wld", wldv),
                                    ("ev", max(0, p.val("ev") - 1))]
                                   if oc == 0 and hb == 0 else []),
                            inc=(("mm", 1) if oc == 7 else None))
                mm_after.append(p.val("mm"))
                mmv = p.val("mm")
                xf = xft[t % 2]
                stw = st_after[t - 2] if t >= 2 else 0
                p.add("vector", lambda e, xf=xf, xn=xnt[t % 2], pp=pp: e.tensor_add(
                    xf[:, 0:512], xn[:, 0:512], pp[:, 0:512]),
                    waits=[("mm", mmv), ("st", stw)])
                p.add("vector", lambda e, xf=xf, xn=xnt[t % 2], pp=pp: e.tensor_add(
                    xf[:, 512:1024], xn[:, 512:1024], pp[:, 512:1024]), inc=("ev", 1))
                ev_after3.append(p.val("ev"))
                evh = p.val("ev")
                p.add("scalar", lambda e, xf=xf, sq=sqt[t % 2]: e.activation(
                    sq[:, :], xf[:, :], AF.Square), waits=[("ev", evh)], inc=("act", 1))
                avq = p.val("act")
                p.add("vector", lambda e, sq=sqt[t % 2]: e.reduce_sum(
                    st2[:, 0:1], sq[:, :], axis=AX.X), waits=[("act", avq)], inc=("dve", 1))
                dv = p.val("dve")
                p.add("vector", lambda e: e.tensor_scalar_mul(st2[:, 0:1], st2[:, 0:1], 1.0 / D),
                      waits=[("dve", dv)], inc=("dve", 1))
                dv = p.val("dve")
                p.add("scalar", lambda e: e.activation(st2[:, 1:2], st2[:, 0:1], AF.Sqrt,
                                                       bias=epsc[:, 0:1]),
                      waits=[("dve", dv)], inc=("act", 1))
                av = p.val("act")
                p.add("vector", lambda e: e.reciprocal(st2[:, 2:3], st2[:, 1:2]),
                      waits=[("act", av)], inc=("dve", 1))
                dv = p.val("dve")
                p.add("vector", lambda e, xf=xf, sq=sqt[t % 2]: e.tensor_scalar_mul(
                    sq[:, :], xf[:, :], st2[:, 2:3]), waits=[("dve", dv)], inc=("dve", 1))
                dv = p.val("dve")
                p.add("vector", lambda e, sq=sqt[t % 2], h2=h2t[t % 2]: e.tensor_tensor(
                    h2[:, :], sq[:, :], ln2b[:, :], ALU.mult), waits=[("dve", dv)],
                    inc=("dve", 1))
                dvv = p.val("dve")
                p.add("sync", lambda e, xf=xf, t=t: e.dma_start(
                    xf_s.ap()[t * 128:(t + 1) * 128, :], xf[:, :]),
                    waits=[("dvv_", 0)] if False else [("dve", dvv)], inc=("st", 16))
                p.add("sync", lambda e, h2=h2t[t % 2], t=t: e.dma_start(
                    h2_s.ap()[t * 128:(t + 1) * 128, :], h2[:, :]), inc=("st", 16))
                st_after.append(p.val("st"))
            p.add("sync", lambda e: e.nop(), waits=[("st", p.val("st"))])
            emit(nc, p, sems)
        if nphase < 6:
            return nc

        with ExitStack() as ps:
            def sbl(n, sh, dt):
                return ps.enter_context(nc.sbuf_tensor(n, sh, dt))
            gate = sbl("gate", [128, 8 * E], BF16)
            h2f = [sbl(f"h2f{i}", [128, 8 * 512], BF16) for i in range(2)]
            lgt = sbl("lgt", [8, S], F32)
            wtm = sbl("wtm", [128, E], F32)
            ltm = sbl("ltm", [128, E], F32)
            stt = sbl("stt", [128, 8], F32)
            wfr = sbl("wfrb", [8, S], F32)
            psl = ps.enter_context(nc.psum_tensor("psl", [8, 512], F32))
            pst_ = ps.enter_context(nc.psum_tensor("pst_", [128, 8], F32))
            psw = ps.enter_context(nc.psum_tensor("psw", [8, 128], F32))
            p = Prog("P4", cnt)
            warm(p, AF.Exp)
            for dc in range(8):
                p.add("gpsimd", lambda e, dc=dc: e.dma_start(
                    gate[:, dc * E:(dc + 1) * E], gateT_h.ap()[dc * 128:(dc + 1) * 128, :]),
                    inc=("wld", 16))
            wldv = p.val("wld")
            for tb in range(NB):
                hb = h2f[tb % 2]
                for dc in range(8):
                    p.add("sync", lambda e, hb=hb, tb=tb, dc=dc: e.dma_start(
                        hb[:, dc * 512:(dc + 1) * 512],
                        h2_s.ap()[tb * 512:(tb + 1) * 512, dc * 128:(dc + 1) * 128],
                        transpose=True),
                        waits=[("mm", max(0, p.val("mm") - 1))], inc=("xld", 16))
                xldv = p.val("xld")
                for dc in range(8):
                    p.add("tensor", lambda e, hb=hb, dc=dc: e.matmul(
                        psl[:, :], gate[:, dc * E:(dc + 1) * E], hb[:, dc * 512:(dc + 1) * 512],
                        start=(dc == 0), stop=(dc == 7)),
                        waits=([("xld", xldv), ("wld", wldv),
                                ("ev", max(0, p.val("ev") - 1))] if dc == 0 else []),
                        inc=(("mm", 1) if dc == 7 else None))
                mmv = p.val("mm")
                p.add("vector", lambda e, tb=tb: e.tensor_copy(
                    lgt[0:8, tb * 512:(tb + 1) * 512], psl[:, :]),
                    waits=[("mm", mmv)], inc=("ev", 1))
            evv = p.val("ev")
            for t in range(NT):
                p.add("tensor", lambda e, t=t: e.transpose(
                    pst_[:, :], lgt[0:8, t * 128:(t + 1) * 128], id32[0:8, 0:8]),
                    waits=[("ev", evv), ("dve", p.val("dve"))], inc=("tp", 1))
                tpv = p.val("tp")
                def ds4(fn, waits=()):
                    w = [("dve", p.val("dve"))] + list(waits)
                    p.add("vector", fn, waits=w, inc=("dve", 1))
                    return p.val("dve")
                ds4(lambda e: e.tensor_copy(ltm[:, :], pst_[:, :]), waits=[("tp", tpv)])
                ds4(lambda e: e.reduce_max(stt[:, 0:1], ltm[:, :], axis=AX.X))
                ds4(lambda e: e.tensor_scalar_mul(stt[:, 4:5], stt[:, 0:1], -1.0))
                ds4(lambda e: e.tensor_scalar(
                    wtm[:, :], ltm[:, :], stt[:, 0:1], None, ALU.is_equal))
                ds4(lambda e: e.scalar_tensor_tensor(
                    ltm[:, :], wtm[:, :], -1e30, ltm[:, :], ALU.mult, ALU.add))
                dv_m2 = ds4(lambda e: e.reduce_max(stt[:, 1:2], ltm[:, :], axis=AX.X))
                p.add("scalar", lambda e: e.activation(stt[:, 2:3], stt[:, 1:2], AF.Exp,
                                                       bias=stt[:, 4:5]),
                      waits=[("dve", dv_m2)], inc=("act", 1))
                av = p.val("act")
                ds4(lambda e: e.tensor_scalar_add(stt[:, 3:4], stt[:, 2:3], 1.0),
                    waits=[("act", av)])
                ds4(lambda e: e.reciprocal(stt[:, 5:6], stt[:, 3:4]))
                ds4(lambda e: e.tensor_mul(stt[:, 6:7], stt[:, 2:3], stt[:, 5:6]))
                ds4(lambda e: e.tensor_scalar(
                    ltm[:, :], ltm[:, :], stt[:, 1:2], None, ALU.is_equal))
                ds4(lambda e: e.tensor_scalar_mul(wtm[:, :], wtm[:, :], stt[:, 5:6]))
                ds4(lambda e: e.tensor_scalar_mul(ltm[:, :], ltm[:, :], stt[:, 6:7]))
                dv = ds4(lambda e: e.tensor_add(wtm[:, :], wtm[:, :], ltm[:, :]))
                p.add("tensor", lambda e, t=t: e.transpose(psw[:, :], wtm[:, :], id32[:, :]),
                      waits=[("dve", dv), ("ev", max(0, p.val("ev") - 1))], inc=("tp", 1))
                tpv = p.val("tp")
                p.add("vector", lambda e, t=t: e.tensor_copy(
                    wfr[0:8, t * 128:(t + 1) * 128], psw[:, :]),
                    waits=[("tp", tpv)], inc=("ev", 1))
            evv = p.val("ev")
            p.add("sync", lambda e: e.dma_start(wfr_s.ap(), wfr[:, :]),
                  waits=[("ev", evv)], inc=("st", 16))
            p.add("sync", lambda e: e.nop(), waits=[("st", p.val("st"))])
            emit(nc, p, sems)
        if nphase < 7:
            return nc

        with ExitStack() as ps:
            def sbl(n, sh, dt):
                return ps.enter_context(nc.sbuf_tensor(n, sh, dt))
            h2q = sbl("h2q", [128, 8 * 1024], BF16)
            gsb = sbl("gsb", [128, 16 * 1024], BF16)
            macc = sbl("macc", [128, 8 * 1024], F32)
            s1b = sbl("s1b", [128, 1024], BF16)
            wfb = [sbl(f"wfb{i}", [128, 1024], F32) for i in range(2)]
            w13b = [sbl(f"w13b{i}", [128, 1024], BF16) for i in range(4)]
            w2sl = [sbl(f"w2sl{i}", [128, 16 * 128], BF16) for i in range(2)]
            wfrl = sbl("wfrl", [8, S], F32)
            acct = [sbl(f"acct{i}", [128, D], F32) for i in range(2)]
            xftl = [sbl(f"xftl{i}", [128, D], F32) for i in range(2)]
            outb = [sbl(f"outb{i}", [128, D], F32) for i in range(2)]
            psy = [ps.enter_context(nc.psum_tensor(f"psy{i}", [128, 1024], F32))
                   for i in range(2)]
            psm = [ps.enter_context(nc.psum_tensor(f"psm{i}", [128, 1024], F32))
                   for i in range(2)]

            p = Prog("P5", cnt)
            warm(p, AF.Silu)
            p.add("sync", lambda e: e.dma_start(wfrl[:, :], wfr_s.ap()), inc=("xld", 16))
            wfr_ld = p.val("xld")
            slab_i = 0
            w13_free = [0, 0, 0, 0]
            w2_free = [0, 0]
            h2q_free = 0
            macc_free = 0
            psy0_free = 0
            psy1_free = 0
            psm_free = [("asm", 0), ("asm", 0)]
            g_prev_m = 0
            s1_free = 0
            for qq in range(4):
                for dc in range(8):
                    p.add("sync", lambda e, qq=qq, dc=dc: e.dma_start(
                        h2q[:, dc * 1024:(dc + 1) * 1024],
                        h2_s.ap()[qq * 1024:(qq + 1) * 1024, dc * 128:(dc + 1) * 128],
                        transpose=True),
                        waits=[("mm", h2q_free)], inc=("xld", 16))
                h2ld = p.val("xld")
                p.add("vector", lambda e: e.memset(macc[:, :], 0.0),
                      waits=[("tp", macc_free)], inc=("dve", 1))
                mz = p.val("dve")
                for ee in range(E):
                    par = ee % 2
                    wf = wfb[par]
                    p.add("tensor", lambda e, ee=ee, qq=qq: e.matmul(
                        psy[0][:, 0:512], sel[0:8, ee * 128:(ee + 1) * 128],
                        wfrl[0:8, qq * 1024: qq * 1024 + 512], start=True, stop=True),
                        waits=[("xld", wfr_ld), ("ex", psy0_free), ("ev", psy1_free)])
                    p.add("tensor", lambda e, ee=ee, qq=qq: e.matmul(
                        psy[1][:, 0:512], sel[0:8, ee * 128:(ee + 1) * 128],
                        wfrl[0:8, qq * 1024 + 512:(qq + 1) * 1024], start=True, stop=True),
                        inc=("mm", 1))
                    mmv = p.val("mm")
                    p.add("vector", lambda e, wf=wf: e.tensor_copy(wf[:, 0:512],
                                                                   psy[0][:, 0:512]),
                          waits=[("mm", mmv)])
                    p.add("vector", lambda e, wf=wf: e.tensor_copy(wf[:, 512:1024],
                                                                   psy[1][:, 0:512]),
                          inc=("ev", 1))
                    wf_ok = p.val("ev")
                    for fc in range(16):
                        sl = [w13b[(slab_i + 0) % 4], w13b[(slab_i + 1) % 4]]
                        for ii, fcc in enumerate((fc, fc + 16)):
                            p.add("gpsimd", lambda e, ee=ee, fcc=fcc, s=sl[ii]: e.dma_start(
                                s.ap().rearrange("p (dc c) -> p dc c", c=128),
                                w13T_h.ap()[ee, :, fcc * 128:(fcc + 1) * 128].rearrange(
                                    "(dc p) c -> p dc c", p=128)),
                                waits=[("mm", w13_free[(slab_i + ii) % 4])], inc=("wld", 16))
                        wldv = p.val("wld")
                        for ii in range(2):
                            pp = psy[ii]
                            for tb in range(2):
                                for dc in range(8):
                                    wts = []
                                    if dc == 0 and tb == 0:
                                        wts = [("wld", wldv), ("xld", h2ld),
                                               ("ev", wf_ok)]
                                        if ii == 0:
                                            wts.append(("ex", psy0_free))
                                        else:
                                            wts.append(("ev", psy1_free))
                                    p.add("tensor", lambda e, pp=pp, s=sl[ii], dc=dc, tb=tb:
                                          e.matmul(pp[:, tb * 512:(tb + 1) * 512],
                                                   s[:, dc * 128:(dc + 1) * 128],
                                                   h2q[:, dc * 1024 + tb * 512:
                                                        dc * 1024 + (tb + 1) * 512],
                                                   start=(dc == 0), stop=(dc == 7)),
                                          waits=wts, inc=(("mm", 1) if dc == 7 else None))
                        w13_free[slab_i % 4] = p.val("mm") - 2
                        w13_free[(slab_i + 1) % 4] = p.val("mm")
                        slab_i += 2
                        mmv = p.val("mm")
                        p.add("scalar", lambda e: e.activation(s1b[:, :], psy[0][:, :],
                                                               AF.Silu),
                              waits=[("mm", mmv - 2), ("gg", s1_free)], inc=("ex", 1))
                        psy0_free = p.val("ex")
                        exv = p.val("ex")
                        gcols = gsb[:, fc * 1024:(fc + 1) * 1024]
                        p.add("vector", lambda e, g=gcols, wf=wf: e.tensor_tensor(
                            g, psy[1][:, :], wf[:, :], ALU.mult),
                            waits=[("mm", mmv), ("ev", wf_ok), ("mm", g_prev_m)],
                            inc=("ev", 1))
                        psy1_free = p.val("ev")
                        p.add("vector", lambda e, g=gcols: e.tensor_tensor(
                            g, g, s1b[:, :], ALU.mult),
                            waits=[("ex", exv), ("ev", psy1_free)], inc=("gg", 1))
                        s1_free = p.val("gg")
                    g_all = p.val("gg")
                    def w2dma(dc2, ee=ee):
                        p.add("gpsimd", lambda e, ee=ee, dc2=dc2: e.dma_start(
                            w2sl[dc2 % 2].ap().rearrange("p (fc c) -> p fc c", c=128),
                            w2T_h.ap()[ee, :, dc2 * 128:(dc2 + 1) * 128].rearrange(
                                "(fc p) c -> p fc c", p=128)),
                            waits=[("mm", w2_free[dc2 % 2])], inc=("wld", 16))
                        return p.val("wld")
                    w2ld_at = {0: w2dma(0), 1: w2dma(1)}
                    for dc2 in range(8):
                        pp = psm[dc2 % 2]
                        for tb in range(2):
                            for fc in range(16):
                                wts = []
                                if fc == 0 and tb == 0:
                                    wts = [("gg", g_all), ("wld", w2ld_at[dc2]),
                                           psm_free[dc2 % 2]]
                                p.add("tensor", lambda e, pp=pp, dc2=dc2, fc=fc,
                                      tb=tb: e.matmul(
                                    pp[:, tb * 512:(tb + 1) * 512],
                                    w2sl[dc2 % 2][:, fc * 128:(fc + 1) * 128],
                                    gsb[:, fc * 1024 + tb * 512:fc * 1024 + (tb + 1) * 512],
                                    start=(fc == 0), stop=(fc == 15)),
                                    waits=wts, inc=(("mm", 1) if fc == 15 else None))
                        w2_free[dc2 % 2] = p.val("mm")
                        if dc2 + 2 < 8:
                            w2ld_at[dc2 + 2] = w2dma(dc2 + 2)
                        mmv = p.val("mm")
                        p.add("vector", lambda e, pp=pp, dc2=dc2: e.tensor_add(
                            macc[:, dc2 * 1024:dc2 * 1024 + 512],
                            macc[:, dc2 * 1024:dc2 * 1024 + 512], pp[:, 0:512]),
                            waits=[("mm", mmv), ("dve", mz)])
                        p.add("vector", lambda e, pp=pp, dc2=dc2: e.tensor_add(
                            macc[:, dc2 * 1024 + 512:(dc2 + 1) * 1024],
                            macc[:, dc2 * 1024 + 512:(dc2 + 1) * 1024], pp[:, 512:1024]),
                            inc=("asm", 1))
                        psm_free[dc2 % 2] = ("asm", p.val("asm"))
                    g_prev_m = p.val("mm")
                h2q_free = p.val("mm")
                asm_all = p.val("asm")
                ot_after = {}
                add_after = {}
                for sub in range(8):
                    t = qq * 8 + sub
                    at = acct[sub % 2]
                    xt = xftl[sub % 2]
                    p.add("sync", lambda e, at=at, t=t: e.dma_start(
                        at[:, :], acc1_s.ap()[t * 128:(t + 1) * 128, :]),
                        waits=[("ev", add_after.get(sub - 2, 0))], inc=("xld", 16))
                    p.add("sync", lambda e, xt=xt, t=t: e.dma_start(
                        xt[:, :], xf_s.ap()[t * 128:(t + 1) * 128, :]), inc=("xld", 16))
                    xldv = p.val("xld")
                    for dc in range(8):
                        p.add("tensor", lambda e, sub=sub, dc=dc: e.transpose(
                            psm[0][:, dc * 128:(dc + 1) * 128],
                            macc[:, dc * 1024 + sub * 128: dc * 1024 + (sub + 1) * 128],
                            id32[:, :]),
                            waits=([("asm", asm_all),
                                    ("ev", add_after.get(sub - 1, 0))]
                                   if dc == 0 else []),
                            inc=(("tp", 1) if dc == 7 else None))
                    tpv = p.val("tp")
                    ob = outb[sub % 2]
                    p.add("vector", lambda e, ob=ob, at=at, xt=xt: e.tensor_add(
                        ob[:, :], at[:, :], xt[:, :]),
                        waits=[("xld", xldv), ("ot", ot_after.get(sub - 2, 0))],
                        inc=("ev", 1))
                    ev1 = p.val("ev")
                    p.add("vector", lambda e, ob=ob: e.tensor_add(
                        ob[:, :], ob[:, :], psm[0][:, :]),
                        waits=[("tp", tpv), ("ev", ev1)], inc=("ev", 1))
                    add_after[sub] = p.val("ev")
                    evv = p.val("ev")
                    p.add("sync", lambda e, ob=ob, t=t: e.dma_start(
                        out_h.ap()[t * 128:(t + 1) * 128, :], ob[:, :]),
                        waits=[("ev", evv)], inc=("ot", 16))
                    ot_after[sub] = p.val("ot")
                macc_free = p.val("tp")
                psm_free[0] = ("ev", p.val("ev"))
            p.add("sync", lambda e: e.nop(), waits=[("ot", p.val("ot"))])
            emit(nc, p, sems)

        for k, v in cnt.items():
            assert v < 60000, (k, v)
    return nc


_NC_CACHE = {}


def _build_cached():
    if "nc" not in _NC_CACHE:
        _NC_CACHE["nc"] = build()
    return _NC_CACHE["nc"]


def kernel(L0_attn_x, L0_ffn_ln1_w, L0_ffn_ln2_w, L0_ffn_wqkv, L0_ffn_wo,
           L0_ffn_gate_w, L0_ffn_w13, L0_ffn_w2):
    import ml_dtypes
    from concourse.bass_utils import run_bass_kernel_spmd
    BF = ml_dtypes.bfloat16
    x = np.asarray(L0_attn_x, dtype=np.float32)
    shared = {
        "wqkvT": np.ascontiguousarray(np.asarray(L0_ffn_wqkv, np.float32).T),
        "woT": np.ascontiguousarray(np.asarray(L0_ffn_wo, np.float32).T),
        "gateT": np.ascontiguousarray(np.asarray(L0_ffn_gate_w, np.float32).T),
        "w13T": np.ascontiguousarray(np.asarray(L0_ffn_w13, np.float32).transpose(0, 2, 1)),
        "w2T": np.ascontiguousarray(np.asarray(L0_ffn_w2, np.float32).transpose(0, 2, 1)),
        "ln1c": np.ascontiguousarray(np.asarray(L0_ffn_ln1_w, np.float32).reshape(8, 128).T),
        "ln2b": np.ascontiguousarray(np.broadcast_to(
            np.asarray(L0_ffn_ln2_w, np.float32)[None, :], (128, 1024))),
        "id32": np.eye(128, dtype=np.float32),
        "triu": np.triu(np.ones((128, 128))).astype(BF),
        "iotar": np.ascontiguousarray(np.broadcast_to(
            np.arange(32, dtype=np.float32)[None, :], (128, 32))),
        "iotac": np.arange(128, dtype=np.float32)[:, None].copy(),
    }
    sel = np.zeros((8, 1024), dtype=np.float32)
    for e in range(8):
        sel[e, e * 128:(e + 1) * 128] = 1.0
    shared["sel"] = sel
    nc = _build_cached()
    in_maps = [dict(shared, x=np.ascontiguousarray(x[c])) for c in range(8)]
    res = run_bass_kernel_spmd(nc, in_maps, core_ids=list(range(8)))
    return np.stack([np.asarray(res.results[c]["out"]) for c in range(8)], axis=0)
